# version 01
# baseline (speedup 1.0000x reference)
import numpy as np
import ml_dtypes
import concourse.bass as bass
import concourse.mybir as mybir
from concourse.bass_utils import run_bass_kernel_spmd

F32 = mybir.dt.float32
BF16 = mybir.dt.bfloat16
I32 = mybir.dt.int32
AF = mybir.ActivationFunctionType
ALU = mybir.AluOpType
AX = mybir.AxisListType

T = 8192
D = 2048
KD = 512
DFF = 1024
NE = 8
HD = 128
NG = 4
HALF = 4096
GRP = 1024
NGRP = HALF // GRP
EPS = 1e-5
LJ = [8, 16, 24, 32, 40, 48, 56, 64]
BIG = 30000.0
BF = ml_dtypes.bfloat16


def own_tiles(c):
    i = c // 2
    if c % 2 == 0:
        return [i, i + 8, i + 16, i + 24, 39 - i, 47 - i, 55 - i, 63 - i]
    return [i + 4, i + 12, i + 20, i + 28, 35 - i, 43 - i, 51 - i, 59 - i]


def rope_tables(positions, scale=1.0):
    freqs = 1.0 / (10000.0 ** (np.arange(0, HD, 2, dtype=np.float64) / HD))
    ang = np.asarray(positions)[None, :].astype(np.float64) * np.concatenate([freqs, freqs])[:, None]
    return (np.cos(ang) * scale).astype(np.float32), (np.sin(ang) * scale).astype(np.float32)


def boundary_masks(c):
    own = own_tiles(c)
    tri = (np.arange(128)[:, None] <= np.arange(128)[None, :]).astype(np.float32)
    out = np.zeros((8, 128, 8 * 512), dtype=np.float32)
    for j in range(8):
        tj = own[j]
        for rel in range(8):
            tau = LJ[j] - 8 + rel
            blk = np.ones((128, 128), np.float32) if tau < tj else (
                tri if tau == tj else np.zeros((128, 128), np.float32))
            out[j, :, rel * 512:(rel + 1) * 512] = np.tile(blk, (1, 4))
    return np.ascontiguousarray(out.astype(BF))


def pmat_host():
    P = np.zeros((128, 128), dtype=np.float32)
    for a in range(64):
        P[a + 64, a] = -1.0
    for a in range(64, 128):
        P[a - 64, a] = 1.0
    return np.ascontiguousarray(P.astype(BF))


class Prog:
    ENGS = ("sync", "gpsimd", "tensor", "scalar", "vector")

    def __init__(self, nc):
        self.nc = nc
        self.ops = {e: [] for e in self.ENGS}
        self.cnt = {}

    def _sem(self, name):
        if name not in self.cnt:
            self.cnt[name] = [self.nc.alloc_semaphore(name), 0]
        return self.cnt[name]

    def val(self, name):
        return self.cnt[name][1] if name in self.cnt else 0

    def on(self, eng, fn, wait=(), inc=()):
        rwait = []
        for (s, v) in wait:
            assert v <= self.val(s), f"wait on future {s}={v} > {self.val(s)}"
            if v > 0:
                rwait.append((self._sem(s)[0], v))
        post, rinc = {}, []
        for (s, n) in inc:
            ent = self._sem(s)
            ent[1] += n
            post[s] = ent[1]
            rinc.append((ent[0], n))
        self.ops[eng].append((fn, tuple(rwait), tuple(rinc)))
        return post

    def allvals(self, *names):
        return [(s, self.val(s)) for s in names if self.val(s) > 0]

    def flush(self, name):
        nc = self.nc
        with nc.Block(name) as block:
            for engname in self.ENGS:
                eng_ops = self.ops[engname]
                if not eng_ops:
                    continue

                def body(eng, eng_ops=eng_ops):
                    for fn, waits, incs in eng_ops:
                        for (h, v) in waits:
                            eng.wait_ge(h, v)
                        inst = fn(eng)
                        for (h, n) in incs:
                            assert inst is not None
                            inst = inst.then_inc(h, n)

                getattr(block, engname)(body)
        self.ops = {e: [] for e in self.ENGS}


def emit_ln(p, x_ap, out_ap, st, sqscratch, epsc, ready=()):
    p.on("vector", lambda e: e.reduce_sum(st[:, 0:1], x_ap, AX.X), wait=ready, inc=[("SV", 1)])
    c = p.on("scalar", lambda e: e.activation(sqscratch, x_ap, AF.Square, accum_out=st[:, 1:2]),
             wait=ready, inc=[("SC", 1)])

    va = p.on("vector", lambda e: e.tensor_scalar_mul(st[:, 2:3], st[:, 0:1], 1.0 / D),
              wait=[("SC", c["SC"])], inc=[("SV", 1)])
    vb = p.on("vector", lambda e: e.tensor_mul(st[:, 7:8], st[:, 2:3], st[:, 2:3]),
              wait=[("SV", va["SV"])], inc=[("SV", 1)])
    v2 = p.on("vector",
              lambda e: e.scalar_tensor_tensor(st[:, 3:4], st[:, 1:2], 1.0 / D, st[:, 7:8],
                                               ALU.mult, ALU.subtract),
              wait=[("SV", vb["SV"])], inc=[("SV", 1)])

    ca = p.on("scalar", lambda e: e.activation(st[:, 6:7], st[:, 3:4], AF.Ln, bias=epsc[:, 0:1]),
              wait=[("SV", v2["SV"])], inc=[("SC", 1)])
    c2 = p.on("scalar", lambda e: e.activation(st[:, 4:5], st[:, 6:7], AF.Exp, scale=-0.5),
              wait=[("SC", ca["SC"])], inc=[("SC", 1)])
    v3 = p.on("vector",
              lambda e: e.scalar_tensor_tensor(st[:, 5:6], st[:, 2:3], -1.0, st[:, 4:5],
                                               ALU.mult, ALU.mult),
              wait=[("SC", c2["SC"])], inc=[("SV", 1)])
    c3 = p.on("scalar",
              lambda e: e.activation(out_ap, x_ap, AF.Identity, bias=st[:, 5:6], scale=st[:, 4:5]),
              wait=[("SV", v3["SV"])], inc=[("SC", 1)])
    return c3["SC"]


def emit_routing(p, xnT, rwT, cbuf, rt, ps_r, m, xnT_ready, prev_read, half_factor=0.5):
    def mms(e):
        r = None
        for k in range(16):
            r = e.matmul(ps_r[:, 0:8], xnT[:, k * 1024 + m * 128: k * 1024 + (m + 1) * 128],
                         rwT[:, k * 8:(k + 1) * 8], start=(k == 0), stop=(k == 15))
        return r
    t = p.on("tensor", mms, wait=list(xnT_ready) + list(prev_read), inc=[("TM", 1)])

    w0 = p.on("vector", lambda e: e.reduce_max(rt[:, 0:1], ps_r[:, 0:8], AX.X),
              wait=[("TM", t["TM"])], inc=[("SV", 1)])
    w1_ = p.on("vector",
               lambda e: e.tensor_scalar(rt[:, 8:16], ps_r[:, 0:8], rt[:, 0:1], None, ALU.is_equal),
               wait=[("SV", w0["SV"])], inc=[("SV", 1)])
    w2_ = p.on("vector",
               lambda e: e.scalar_tensor_tensor(rt[:, 16:24], rt[:, 8:16], -BIG, ps_r[:, 0:8],
                                                ALU.mult, ALU.add),
               wait=[("SV", w1_["SV"])], inc=[("SV", 1)])
    w3_ = p.on("vector", lambda e: e.reduce_max(rt[:, 1:2], rt[:, 16:24], AX.X),
               wait=[("SV", w2_["SV"])], inc=[("SV", 1)])
    w4_ = p.on("vector",
               lambda e: e.tensor_scalar(rt[:, 24:32], ps_r[:, 0:8], rt[:, 1:2], None, ALU.is_ge),
               wait=[("SV", w3_["SV"])], inc=[("SV", 1)])
    v1 = p.on("vector", lambda e: e.tensor_scalar_mul(rt[:, 2:3], rt[:, 0:1], -1.0),
              wait=[("SV", w0["SV"])], inc=[("SV", 1)])
    c1 = p.on("scalar",
              lambda e: e.activation(rt[:, 32:40], ps_r[:, 0:8], AF.Exp, bias=rt[:, 2:3]),
              wait=[("SV", v1["SV"])], inc=[("SC", 1)])

    u0 = p.on("vector", lambda e: e.tensor_mul(rt[:, 32:40], rt[:, 32:40], rt[:, 24:32]),
              wait=[("SC", c1["SC"]), ("SV", w4_["SV"])], inc=[("SV", 1)])
    u1 = p.on("vector", lambda e: e.reduce_sum(rt[:, 3:4], rt[:, 32:40], AX.X),
              wait=[("SV", u0["SV"])], inc=[("SV", 1)])
    u2 = p.on("vector", lambda e: e.reciprocal(rt[:, 4:5], rt[:, 3:4]),
              wait=[("SV", u1["SV"])], inc=[("SV", 1)])
    v2 = p.on("vector",
              lambda e: e.tensor_scalar(cbuf[:, 8 * m:8 * m + 8], rt[:, 32:40], rt[:, 4:5],
                                        half_factor, ALU.mult, ALU.mult),
              wait=[("SV", u2["SV"])], inc=[("SV", 1)])
    return v2["SV"], [("SC", c1["SC"]), ("SV", v2["SV"])]


def emit_moe(p, sb, ps, xnT, cbuf, resid, w1d, w2d, w3d, xnT_ready, c_ready, o0_guard):
    wslots = [sb["wA"], sb["wB"]]
    w3slots = [sb["w3s0"], sb["w3s1"]]
    wl = {}
    slot_free = {}
    s1_read = {}
    hp_read = ("TM", 0)

    def sched_load(kind, e, nd=None):
        if kind == "w3":
            slot = w3slots[(e * 8 + nd) % 2]
            src_ = w3d[e, :, nd * 256:(nd + 1) * 256].rearrange("(f pp) n -> pp f n", pp=128)
            dst = slot[:, :].rearrange("pp (f n) -> pp f n", f=8)
            key = ("w3", (e * 8 + nd) % 2)
        else:
            slot = wslots[0 if kind == "w1" else 1]
            src_ = (w1d if kind == "w1" else w2d)[e].rearrange("(k pp) n -> pp k n", pp=128)
            dst = slot[:, :].rearrange("pp (k n) -> pp k n", k=16)
            key = ("w", 0 if kind == "w1" else 1)
        waits = [slot_free[key]] if key in slot_free else []
        r = p.on("gpsimd", lambda eng, s=src_, d=dst: eng.dma_start(out=d, in_=s),
                 wait=waits, inc=[("WL", 16)])
        wl[(kind, e, nd)] = r["WL"]

    sched_load("w1", 0)
    sched_load("w2", 0)
    sched_load("w3", 0, 0)
    sched_load("w3", 0, 1)

    hfree = {0: ("TM", 0), 1: ("TM", 0)}
    ofree = {0: o0_guard, 1: ("TM", 0), 2: ("TM", 0), 3: ("TM", 0)}

    for e in range(NE):
        wsl1, wsl2 = wslots
        last_h1 = last_h2 = None
        for tg in range(2):
            for f in range(8):
                par = (tg * 8 + f) % 2
                psx = ps["h0"] if par == 0 else ps["h1"]

                def h1mm(eng, psx=psx, f=f, tg=tg, wsl1=wsl1):
                    r = None
                    for k in range(16):
                        r = eng.matmul(psx[:, 0:512],
                                       wsl1[:, k * 1024 + f * 128:k * 1024 + (f + 1) * 128],
                                       xnT[:, k * 1024 + tg * 512:k * 1024 + tg * 512 + 512],
                                       start=(k == 0), stop=(k == 15))
                    return r
                t = p.on("tensor", h1mm,
                         wait=[("WL", wl[("w1", e, None)]), hfree[par]] + list(xnT_ready),
                         inc=[("TM", 1)])
                last_h1 = t["TM"]
                swaits = [("TM", t["TM"])]
                if (f, tg) in s1_read:
                    swaits.append(s1_read[(f, tg)])
                c = p.on("scalar",
                         lambda eng, psx=psx, f=f:
                         eng.activation(sb["s1"][:, f * 512:(f + 1) * 512], psx[:, 0:512], AF.Silu),
                         wait=swaits, inc=[("SC", 1)])
                hfree[par] = ("SC", c["SC"])
            for f in range(8):
                par = (tg * 8 + f) % 2
                psx = ps["h0"] if par == 0 else ps["h1"]

                def h2mm(eng, psx=psx, f=f, tg=tg, wsl2=wsl2):
                    r = None
                    for k in range(16):
                        r = eng.matmul(psx[:, 0:512],
                                       wsl2[:, k * 1024 + f * 128:k * 1024 + (f + 1) * 128],
                                       xnT[:, k * 1024 + tg * 512:k * 1024 + tg * 512 + 512],
                                       start=(k == 0), stop=(k == 15))
                    return r
                t = p.on("tensor", h2mm,
                         wait=[("WL", wl[("w2", e, None)]), hfree[par], hp_read],
                         inc=[("TM", 1)])
                last_h2 = t["TM"]
                v = p.on("vector",
                         lambda eng, psx=psx, f=f, tg=tg:
                         eng.tensor_mul(sb["hp"][:, f * 1024 + tg * 512:f * 1024 + tg * 512 + 512],
                                        sb["s1"][:, f * 512:(f + 1) * 512], psx[:, 0:512]),
                         wait=[("TM", t["TM"]), ("SC", p.val("SC"))], inc=[("SV", 1)])
                hfree[par] = ("SV", v["SV"])
                s1_read[(f, tg)] = ("SV", v["SV"])
        slot_free[("w", 0)] = ("TM", last_h1)
        slot_free[("w", 1)] = ("TM", last_h2)
        if e + 1 < NE:
            sched_load("w1", e + 1)
            sched_load("w2", e + 1)
        hp_ready = ("SV", p.val("SV"))
        last_o = None
        for nd in range(8):
            w3sl = w3slots[(e * 8 + nd) % 2]
            for m in range(8):
                oslot = (m + nd) % 4
                pso = ps["o"][oslot]

                def omm(eng, pso=pso, m=m, w3sl=w3sl):
                    r = None
                    for f in range(8):
                        r = eng.matmul(pso[:, 0:256],
                                       sb["hp"][:, f * 1024 + m * 128:f * 1024 + (m + 1) * 128],
                                       w3sl[:, f * 256:(f + 1) * 256],
                                       start=(f == 0), stop=(f == 7))
                    return r
                t = p.on("tensor", omm,
                         wait=[("WL", wl[("w3", e, nd)]), hp_ready, ofree[oslot]],
                         inc=[("TM", 1)])
                last_o = t["TM"]
                v = p.on("vector",
                         lambda eng, pso=pso, m=m, nd=nd, e=e:
                         eng.scalar_tensor_tensor(
                             resid[:, m * 2048 + nd * 256:m * 2048 + nd * 256 + 256],
                             pso[:, 0:256], cbuf[:, 8 * m + e:8 * m + e + 1],
                             resid[:, m * 2048 + nd * 256:m * 2048 + nd * 256 + 256],
                             ALU.mult, ALU.add),
                         wait=[("TM", t["TM"])] + (list(c_ready) if e == 0 else []),
                         inc=[("SV", 1)])
                ofree[oslot] = ("SV", v["SV"])
            slot_free[("w3", (e * 8 + nd) % 2)] = ("TM", last_o)
            if nd + 2 < 8:
                sched_load("w3", e, nd + 2)
            elif e + 1 < NE:
                sched_load("w3", e + 1, nd + 2 - 8)
        hp_read = ("TM", last_o)
    return p.val("SV")


def half_write(p, eng, name, non_s, pairs, wait=()):
    npairs = len(pairs)

    ent = p._sem("DW")
    semh = ent[0]
    ent[1] += 16 * (2 * npairs - 1)

    def fn(e):
        with e.register(name) as rh:
            e.reg_load(rh, non_s[0:1, 1:2])
            rv = e.snap(rh, min_val=0, max_val=1)
            c0 = rv < 1
            c1 = rv > 0
            rr = None
            for i, (d0, d1, s_) in enumerate(pairs):
                rr = e.dma_start(out=d0, in_=s_, cond=c0).then_inc(semh, 16)
                rr = e.dma_start(out=d1, in_=s_, cond=c1)
                if i + 1 < npairs:
                    rr = rr.then_inc(semh, 16)
            return rr
    return p.on(eng, fn, wait=wait, inc=[("DW", 16)])


def dyn_write(p, eng, name, non_s, make_dst, src_ap, base, count, wait=(), incsem="DW"):
    def fn(e):
        with e.register(name) as rh:
            e.reg_load(rh, non_s[0:1, 1:2])
            rv = e.snap(rh, min_val=0, max_val=1)
            off = e.snap(rv * HALF + base, min_val=0, max_val=T)
            return e.dma_start(out=make_dst(off), in_=src_ap)
    return p.on(eng, fn, wait=wait, inc=[(incsem, 16)])


def build_program(stages=1, debug=False):
    nc = bass.Bass("TRN2", target_bir_lowering=False, num_devices=8)
    p = Prog(nc)

    xh = nc.dram_tensor("x_half", [HALF, D], F32, kind="ExternalInput").ap()
    wq = nc.dram_tensor("wq", [D, D], F32, kind="ExternalInput").ap()
    wk = nc.dram_tensor("wk", [D, KD], F32, kind="ExternalInput").ap()
    wv = nc.dram_tensor("wv", [D, KD], F32, kind="ExternalInput").ap()
    wo = nc.dram_tensor("wo", [D, D], F32, kind="ExternalInput").ap()
    rw = nc.dram_tensor("rw", [NE, D], F32, kind="ExternalInput").ap()
    w1d = nc.dram_tensor("w1", [NE, D, DFF], F32, kind="ExternalInput").ap()
    w2d = nc.dram_tensor("w2", [NE, D, DFF], F32, kind="ExternalInput").ap()
    w3d = nc.dram_tensor("w3", [NE, DFF, D], F32, kind="ExternalInput").ap()
    cosk = nc.dram_tensor("cosk", [HD, HALF], F32, kind="ExternalInput").ap()
    sink = nc.dram_tensor("sink", [HD, HALF], F32, kind="ExternalInput").ap()
    cosq = nc.dram_tensor("cosq", [HD, GRP], F32, kind="ExternalInput").ap()
    sinq = nc.dram_tensor("sinq", [HD, GRP], F32, kind="ExternalInput").ap()
    bmask = nc.dram_tensor("bmask", [8, 128, 8 * 512], BF16, kind="ExternalInput").ap()
    consts = nc.dram_tensor("consts", [128, 130], BF16, kind="ExternalInput").ap()
    pmatd = nc.dram_tensor("pmat", [128, 128], BF16, kind="ExternalInput").ap()
    cqd = nc.dram_tensor("cqf", [128, GRP], BF16, kind="ExternalInput").ap()
    sqd = nc.dram_tensor("sqf", [128, GRP], BF16, kind="ExternalInput").ap()
    nonce = nc.dram_tensor("nonce", [1, 1], I32, kind="ExternalInput").ap()
    ownt = nc.dram_tensor("ownt", [1, 8], I32, kind="ExternalInput").ap()
    ownoff = nc.dram_tensor("ownoff", [1, 8], I32, kind="ExternalInput").ap()
    halfsel = nc.dram_tensor("halfsel", [1, 1], I32, kind="ExternalInput").ap()

    yout = nc.dram_tensor("yout", [GRP, D], F32, kind="ExternalOutput").ap()

    y1_sh = nc.dram_tensor("y1_sh", [2, NGRP, GRP, D], F32, addr_space="Shared").ap()
    kT_sh = nc.dram_tensor("kT_sh", [2, NGRP, KD, GRP], BF16, addr_space="Shared").ap()
    v_sh = nc.dram_tensor("v_sh", [2, NGRP, GRP, KD], BF16, addr_space="Shared").ap()
    y1_loc = nc.dram_tensor("y1_loc", [NGRP, GRP, D], F32).ap()
    kT_loc = nc.dram_tensor("kT_loc", [NGRP, KD, GRP], BF16).ap()
    v_loc = nc.dram_tensor("v_loc", [NGRP, GRP, KD], BF16).ap()
    gstat_sh = nc.dram_tensor("gstat_sh", [2, 64], F32, addr_space="Shared").ap()
    flags_sh = nc.dram_tensor("flags_sh", [1, 16], I32, addr_space="Shared").ap()
    xn_b = nc.dram_tensor("xn_b", [GRP, D], BF16).ap()
    r_stage = nc.dram_tensor("r_stage", [GRP, D], F32).ap()

    dbg = {}
    if debug:
        dbg["y1"] = nc.dram_tensor("dbg_y1", [HALF, D], F32, kind="ExternalOutput").ap()
        dbg["kT"] = nc.dram_tensor("dbg_kT", [KD, HALF], BF16, kind="ExternalOutput").ap()
        dbg["v"] = nc.dram_tensor("dbg_v", [HALF, KD], BF16, kind="ExternalOutput").ap()
        dbg["knrm"] = nc.dram_tensor("dbg_knrm", [1, 8], F32, kind="ExternalOutput").ap()
        dbg["cbuf"] = nc.dram_tensor("dbg_cbuf", [128, 256], F32, kind="ExternalOutput").ap()
        dbg["hp"] = nc.dram_tensor("dbg_hp", [128, 8192], BF16, kind="ExternalOutput").ap()
        dbg["y2"] = nc.dram_tensor("dbg_y2", [GRP, D], F32, kind="ExternalOutput").ap()
        dbg["s1"] = nc.dram_tensor("dbg_s1", [128, 4096], BF16, kind="ExternalOutput").ap()

    pers = [
        nc.sbuf_tensor("ident", [128, 128], BF16),
        nc.sbuf_tensor("onesc", [128, 1], BF16),
        nc.sbuf_tensor("rwT", [128, 128], BF16),
        nc.sbuf_tensor("non_s", [1, 2], I32),
        nc.sbuf_tensor("knrm", [1, 8], F32),
        nc.sbuf_tensor("epsc", [128, 1], F32),
        nc.sbuf_tensor("onesr", [1, 128], BF16),
    ]
    ident, onesc, rwT, non_s, knrm, epsc, onesr = [t.__enter__() for t in pers]

    p.on("sync", lambda e: e.dma_start(out=ident[:, :], in_=consts[:, 0:128]), inc=[("DX", 16)])
    p.on("vector", lambda e: e.memset(onesc[:, :], 1.0), inc=[("SV", 1)])
    p.on("vector", lambda e: e.memset(epsc[:, :], EPS), inc=[("SV", 1)])
    p.on("vector", lambda e: e.memset(onesr[:, :], 1.0), inc=[("SV", 1)])
    p.on("sync", lambda e: e.dma_start(out=non_s[0:1, 0:1], in_=nonce), inc=[("DX", 16)])
    p.on("sync", lambda e: e.dma_start(out=non_s[0:1, 1:2], in_=halfsel), inc=[("DX", 16)])

    with (
        nc.sbuf_tensor("rwbf", [8, 2048], BF16) as rwbf,
        nc.psum_tensor("psrw", [128, 128], BF16) as psrw,
    ):
        g = p.on("gpsimd", lambda e: e.dma_start(out=rwbf[:, :], in_=rw), inc=[("DW", 16)])

        def trs(e):
            r = None
            for k in range(16):
                r = e.transpose(psrw[:, k * 8:(k + 1) * 8], rwbf[0:8, k * 128:(k + 1) * 128],
                                ident[0:8, 0:8])
            return r
        t = p.on("tensor", trs, wait=[("DW", g["DW"]), ("DX", 16)], inc=[("TM", 1)])
        p.on("vector", lambda e: e.tensor_copy(rwT[:, :], psrw[:, :]),
             wait=[("TM", t["TM"])], inc=[("SV", 1)])
        p.flush("prologue")

    stA = [
        nc.sbuf_tensor("xg", [128, 8 * D], F32),
        nc.sbuf_tensor("xnb", [128, 2 * D], BF16),
        nc.sbuf_tensor("xnT", [128, 16 * 1024], BF16),
        nc.sbuf_tensor("lnst", [128, 16], F32),
        nc.sbuf_tensor("rt", [128, 40], F32),
        nc.sbuf_tensor("cbuf", [128, 64], F32),
        nc.sbuf_tensor("wA", [128, 16 * 1024], BF16),
        nc.sbuf_tensor("wB", [128, 16 * 1024], BF16),
        nc.sbuf_tensor("w3s0", [128, 8 * 256], BF16),
        nc.sbuf_tensor("w3s1", [128, 8 * 256], BF16),
        nc.sbuf_tensor("s1", [128, 8 * 512], BF16),
        nc.sbuf_tensor("hp", [128, 8 * 1024], BF16),
        nc.sbuf_tensor("kbf", [64, 2048], BF16),
    ]
    (xg, xnb, xnT, lnst, rt, cbuf, wA, wB, w3s0, w3s1, s1, hp,
     kbf) = [t.__enter__() for t in stA]
    s1f = s1[:, :].bitcast(F32)
    ktmp_a = s1f[0:64, 0:1024]
    ktmp_b = s1f[0:64, 1024:2048]
    vbf = s1[:, 0:512]
    sq64 = hp[0:64, 4096:6144]
    hpf = hp[:, :].bitcast(F32)
    cosg = hpf[:, 0:1024]
    sing = hpf[:, 1024:2048]
    psA = [
        nc.psum_tensor("ph0", [128, 512], F32),
        nc.psum_tensor("ph1", [128, 512], F32),
        nc.psum_tensor("po0", [128, 512], F32),
        nc.psum_tensor("po1", [128, 512], F32),
        nc.psum_tensor("po2", [128, 512], F32),
        nc.psum_tensor("po3", [128, 512], F32),
        nc.psum_tensor("pk", [128, 1024], F32),
    ]
    ph0, ph1, po0, po1, po2, po3, pk = [t.__enter__() for t in psA]
    sbA = dict(wA=wA, wB=wB, w3s0=w3s0, w3s1=w3s1, s1=s1, hp=hp)
    psd = dict(h0=ph0, h1=ph1, o=[po0, po1, po2, po3])

    for q in range(NGRP):
        goff = q * GRP
        src = xh[goff:goff + GRP, :].rearrange("(m pp) c -> pp m c", pp=128)
        dstx = xg[:, :].rearrange("pp (m c) -> pp m c", m=8)
        dx = p.on("sync", lambda e, s=src, d=dstx: e.dma_start(out=d, in_=s), inc=[("DX", 16)])
        xrdy = [("DX", dx["DX"])]
        xnb_free = {0: 0, 1: 0}
        for m in range(8):
            par = m % 2
            sc = emit_ln(p, xg[:, m * D:(m + 1) * D], xnb[:, par * D:(par + 1) * D], lnst,
                         xnb[:, par * D:(par + 1) * D], epsc,
                         ready=list(xrdy) + ([("DX", xnb_free[par])] if xnb_free[par] else []))
            bb = p.on("sync", lambda e, m=m, par=par: e.dma_start(
                out=xn_b[m * 128:(m + 1) * 128, :], in_=xnb[:, par * D:(par + 1) * D]),
                wait=[("SC", sc)], inc=[("DX", 16)])
            xnb_free[par] = bb["DX"]
        b = bb
        for k in range(16):
            r = p.on("sync",
                     lambda e, k=k: e.dma_start_transpose(xnT[:, k * 1024:(k + 1) * 1024],
                                                          xn_b[:, k * 128:(k + 1) * 128]),
                     wait=[("DX", b["DX"])], inc=[("DX", 16)])
        xnT_ready = [("DX", r["DX"])]
        prev_read = []
        for m in range(8):
            cv, prev_read = emit_routing(p, xnT, rwT, cbuf, rt, po0, m, xnT_ready, prev_read)
        c_ready = [("SV", cv)]
        o0_guard = prev_read[0]
        if debug:
            p.on("sync", lambda e, q=q: e.dma_start(out=dbg["cbuf"][:, q * 64:(q + 1) * 64],
                                                    in_=cbuf[:, :]),
                 wait=[("SV", cv)], inc=[("DX", 16)])
        emit_moe(p, sbA, psd, xnT, cbuf, xg, w1d, w2d, w3d, xnT_ready, c_ready, o0_guard)
        y1_done = p.val("SV")
        if debug and q == 0:
            p.on("sync", lambda e: e.dma_start(out=dbg["hp"], in_=hp[:, :]),
                 wait=[("SV", y1_done)], inc=[("DX", 16)])
            p.on("sync", lambda e: e.dma_start(out=dbg["s1"], in_=s1[:, :]),
                 wait=[("SV", y1_done)], inc=[("DX", 16)])
        for m in range(8):
            yw = p.on("gpsimd",
                      lambda e, q=q, m=m: e.dma_start(out=y1_loc[q, m * 128:(m + 1) * 128, :],
                                                      in_=xg[:, m * D:(m + 1) * D]),
                      wait=([("SV", y1_done)] if m == 0 else []), inc=[("DW", 16)])
        if debug:
            srcy = xg[:, :].rearrange("pp (m c) -> pp m c", m=8)
            dd = dbg["y1"][goff:goff + GRP, :].rearrange("(m pp) c -> pp m c", pp=128)
            p.on("sync", lambda e, d=dd, s=srcy: e.dma_start(out=d, in_=s),
                 wait=[("SV", y1_done)], inc=[("DX", 16)])
        ln2rdy = [("SV", y1_done), ("DX", p.val("DX"))]
        for m in range(8):
            par = m % 2
            sc = emit_ln(p, xg[:, m * D:(m + 1) * D], xnb[:, par * D:(par + 1) * D], lnst,
                         xnb[:, par * D:(par + 1) * D], epsc,
                         ready=list(ln2rdy) + ([("DX", xnb_free[par])] if xnb_free[par] else []))
            ln2rdy = []
            bb = p.on("sync", lambda e, m=m, par=par: e.dma_start(
                out=xn_b[m * 128:(m + 1) * 128, :], in_=xnb[:, par * D:(par + 1) * D]),
                wait=[("SC", sc)], inc=[("DX", 16)])
            xnb_free[par] = bb["DX"]
        b2 = bb
        xnT_guard = ("TM", p.val("TM"))
        for k in range(16):
            r = p.on("sync",
                     lambda e, k=k: e.dma_start_transpose(xnT[:, k * 1024:(k + 1) * 1024],
                                                          xn_b[:, k * 128:(k + 1) * 128]),
                     wait=[("DX", b2["DX"]), xnT_guard], inc=[("DX", 16)])
        xn2T_ready = ("DX", r["DX"])

        wfree = ("TM", p.val("TM"))
        lk = p.on("gpsimd",
                  lambda e: e.dma_start(out=wA[:, 0:16 * 512].rearrange("pp (k n) -> pp k n", k=16),
                                        in_=wk.rearrange("(k pp) n -> pp k n", pp=128)),
                  wait=[wfree], inc=[("WL", 16)])
        lv = p.on("gpsimd",
                  lambda e: e.dma_start(out=wB[:, 0:16 * 512].rearrange("pp (k n) -> pp k n", k=16),
                                        in_=wv.rearrange("(k pp) n -> pp k n", pp=128)),
                  wait=[wfree], inc=[("WL", 16)])
        hp_done = ("TM", p.val("TM"))
        ck = p.on("sync", lambda e, goff=goff: e.dma_start(out=cosg[:, :], in_=cosk[:, goff:goff + GRP]),
                  wait=[hp_done], inc=[("DX", 16)])
        sk = p.on("sync", lambda e, goff=goff: e.dma_start(out=sing[:, :], in_=sink[:, goff:goff + GRP]),
                  wait=[hp_done], inc=[("DX", 16)])
        kbf_free = []
        sqb_free = ("TM", 0)
        for g in range(NG):
            def kmm(e, g=g):
                rr = None
                for n in range(2):
                    for k in range(16):
                        rr = e.matmul(pk[:, n * 512:(n + 1) * 512],
                                      wA[:, k * 512 + g * 128:k * 512 + (g + 1) * 128],
                                      xnT[:, k * 1024 + n * 512:k * 1024 + n * 512 + 512],
                                      start=(k == 0), stop=(k == 15))
                return rr
            t = p.on("tensor", kmm,
                     wait=[("WL", lk["WL"]), xn2T_ready, ("SV", p.val("SV"))], inc=[("TM", 1)])

            r0a = p.on("vector", lambda e: e.tensor_mul(ktmp_a, pk[0:64, :], cosg[0:64, :]),
                       wait=[("TM", t["TM"]), ("DX", ck["DX"]), ("DX", sk["DX"])] + list(kbf_free),
                       inc=[("SV", 1)])
            r0b = p.on("vector", lambda e: e.tensor_mul(ktmp_b, pk[64:128, :], sing[0:64, :]),
                       inc=[("SV", 1)])
            r1 = p.on("vector", lambda e: e.tensor_sub(kbf[:, 0:1024], ktmp_a, ktmp_b),
                      wait=[("SV", r0b["SV"])], inc=[("SV", 1)])
            r2a = p.on("vector", lambda e: e.tensor_mul(ktmp_a, pk[64:128, :], cosg[64:128, :]),
                       wait=[("SV", r1["SV"])], inc=[("SV", 1)])
            r2b = p.on("vector", lambda e: e.tensor_mul(ktmp_b, pk[0:64, :], sing[64:128, :]),
                       inc=[("SV", 1)])
            rv_ = p.on("vector", lambda e: e.tensor_add(kbf[:, 1024:2048], ktmp_a, ktmp_b),
                       wait=[("SV", r2b["SV"])], inc=[("SV", 1)])
            c = p.on("scalar", lambda e: e.activation(sq64, kbf[:, :], AF.Square),
                     wait=[("SV", rv_["SV"]), sqb_free], inc=[("SC", 1)])

            def nmm(e):
                e.matmul(ph0[0:1, 0:512], onesc[0:64, :], sq64[:, 0:512], start=True, stop=False)
                e.matmul(ph0[0:1, 0:512], onesc[0:64, :], sq64[:, 1024:1536], start=False, stop=True)
                e.matmul(ph1[0:1, 0:512], onesc[0:64, :], sq64[:, 512:1024], start=True, stop=False)
                return e.matmul(ph1[0:1, 0:512], onesc[0:64, :], sq64[:, 1536:2048],
                                start=False, stop=True)
            tn = p.on("tensor", nmm, wait=[("SC", c["SC"]), ("SV", p.val("SV"))], inc=[("TM", 1)])
            sqb_free = ("TM", tn["TM"])

            n0 = p.on("vector", lambda e: e.reduce_max(rt[0:1, 0:1], ph0[0:1, 0:512], AX.X),
                      wait=[("TM", tn["TM"])], inc=[("SV", 1)])
            n1 = p.on("vector", lambda e: e.reduce_max(rt[0:1, 1:2], ph1[0:1, 0:512], AX.X),
                      inc=[("SV", 1)])
            n2 = p.on("vector", lambda e: e.tensor_max(rt[0:1, 0:1], rt[0:1, 0:1], rt[0:1, 1:2]),
                      wait=[("SV", n1["SV"])], inc=[("SV", 1)])
            if q == 0:
                p.on("vector", lambda e, g=g: e.tensor_copy(knrm[0:1, g:g + 1], rt[0:1, 0:1]),
                     wait=[("SV", n2["SV"])], inc=[("SV", 1)])
            else:
                p.on("vector",
                     lambda e, g=g: e.tensor_max(knrm[0:1, g:g + 1], knrm[0:1, g:g + 1],
                                                 rt[0:1, 0:1]),
                     wait=[("SV", n2["SV"])], inc=[("SV", 1)])
            kdst = kT_loc[q, g * 128:(g + 1) * 128, :].rearrange("(h p) c -> p h c", h=2)
            ksrc = kbf[:, :].rearrange("p (h c) -> p h c", h=2)
            kw = p.on("gpsimd",
                      lambda e, d=kdst, s=ksrc: e.dma_start(out=d, in_=s),
                      wait=[("SV", rv_["SV"])], inc=[("DW", 16)])
            kbf_free = [("DW", kw["DW"])]
            if debug:
                dkd = dbg["kT"][g * 128:(g + 1) * 128, goff:goff + GRP].rearrange(
                    "(h p) c -> p h c", h=2)
                dk = p.on("sync", lambda e, d=dkd, s=ksrc: e.dma_start(out=d, in_=s),
                    wait=[("SV", rv_["SV"])], inc=[("DX", 16)])
                kbf_free.append(("DX", dk["DX"]))
        vbf_free = list(kbf_free)
        for m in range(8):
            psv = ph0 if m % 2 == 0 else ph1

            def vmm(e, m=m, psv=psv):
                rr = None
                for k in range(16):
                    rr = e.matmul(psv[:, :],
                                  xnT[:, k * 1024 + m * 128:k * 1024 + (m + 1) * 128],
                                  wB[:, k * 512:(k + 1) * 512],
                                  start=(k == 0), stop=(k == 15))
                return rr
            t = p.on("tensor", vmm,
                     wait=[("WL", lv["WL"]), xn2T_ready, ("SV", p.val("SV")),
                           ("SC", p.val("SC"))], inc=[("TM", 1)])
            c = p.on("scalar", lambda e, psv=psv: e.activation(vbf[:, :], psv[:, :], AF.Copy),
                     wait=[("TM", t["TM"])] + list(vbf_free), inc=[("SC", 1)])
            vw = p.on("gpsimd",
                      lambda e, q=q, m=m: e.dma_start(out=v_loc[q, m * 128:(m + 1) * 128, :],
                                                      in_=vbf[:, :]),
                      wait=[("SC", c["SC"])], inc=[("DW", 16)])
            vbf_free = [("DW", vw["DW"])]
            if debug:
                dv = p.on("sync", lambda e, m=m, goff=goff: e.dma_start(
                    out=dbg["v"][goff + m * 128:goff + (m + 1) * 128, :], in_=vbf[:, :]),
                    wait=[("SC", c["SC"])], inc=[("DX", 16)])
                vbf_free.append(("DX", dv["DX"]))
        p.flush("stageA_g%d" % q)

    blob_wait = [("DW", p.val("DW")), ("DX", p.val("DX")), ("SV", p.val("SV"))]
    half_write(p, "gpsimd", "rb", non_s, [
        (y1_sh[0].flatten()[None, :], y1_sh[1].flatten()[None, :], y1_loc.flatten()[None, :]),
        (kT_sh[0].flatten()[None, :], kT_sh[1].flatten()[None, :], kT_loc.flatten()[None, :]),
        (v_sh[0].flatten()[None, :], v_sh[1].flatten()[None, :], v_loc.flatten()[None, :]),
    ], wait=blob_wait)
    gv = half_write(p, "gpsimd", "rg", non_s,
                    [(gstat_sh[0:1, 0:8], gstat_sh[1:2, 0:8], knrm[0:1, :])],
                    wait=[("SV", p.val("SV"))])
    if debug:
        p.on("sync", lambda e: e.dma_start(out=dbg["knrm"], in_=knrm[0:1, :]),
             wait=[("SV", p.val("SV"))], inc=[("DX", 16)])

    fw = half_write(p, "gpsimd", "rf", non_s,
                    [(flags_sh[0:1, 0:1], flags_sh[0:1, 1:2], non_s[0:1, 0:1])],
                    wait=[("DW", p.val("DW")), ("DX", p.val("DX"))])

    def poll(e):
        with (e.register("r_n") as r_n, e.register("r_a") as r_a,
              e.register("r_b") as r_b, e.register("r_go") as r_go):
            e.reg_load(r_n, non_s[0:1, 0:1])
            e.reg_add(r_n, r_n, r_n)
            e.reg_mov(r_go, 1)
            with e.While(r_go):
                e.reg_load(r_a, flags_sh[0:1, 0:1])
                e.reg_load(r_b, flags_sh[0:1, 1:2])
                e.reg_add(r_a, r_a, r_b)
                e.reg_sub(r_go, r_a, r_n)
        return e.engine_nop()
    p.on("gpsimd", poll, wait=[("DW", fw["DW"])], inc=[("BAR", 1)])
    p.flush("barrier")

    for t in reversed(psA):
        t.__exit__(None, None, None)
    for t in reversed(stA):
        t.__exit__(None, None, None)

    if stages >= 2:
        stB_R = [nc.sbuf_tensor("Rr", [128, 8 * D], F32)]
        (Rr,) = [t.__enter__() for t in stB_R]
        stB = [
            nc.sbuf_tensor("qTg", [128, 16 * 1024], BF16),
            nc.sbuf_tensor("outT", [128, 16 * 1024], BF16),
            nc.sbuf_tensor("cqf_s", [128, 1024], BF16),
            nc.sbuf_tensor("sqf_s", [128, 1024], BF16),
            nc.sbuf_tensor("pmat_s", [128, 128], BF16),
            nc.sbuf_tensor("negmv", [128, 32], F32),
            nc.sbuf_tensor("srow", [1, 64], F32),
            nc.sbuf_tensor("gkn", [1, 16], F32),
            nc.sbuf_tensor("ownv", [1, 8], I32),
        ]
        (qTg, outT, cqf_s, sqf_s, pmat_s, negmv, srow, gkn, ownv) = \
            [t.__enter__() for t in stB]
        psB = [
            nc.psum_tensor("SP0", [128, 1024], F32),
            nc.psum_tensor("SP1", [128, 1024], F32),
            nc.psum_tensor("OT", [128, 512], F32),
            nc.psum_tensor("L0", [1, 512], F32),
            nc.psum_tensor("L1", [1, 512], F32),
        ]
        SP0, SP1, OT, L0, L1 = [t.__enter__() for t in psB]
        SP = [SP0, SP1]
        Lp = [L0, L1]

        p.on("sync", lambda e: e.dma_start(out=ownv[:, :], in_=ownoff), inc=[("DX", 16)])
        g1 = p.on("sync", lambda e: e.dma_start(out=gkn[0:1, 0:8], in_=gstat_sh[0:1, 0:8]),
                  inc=[("DX", 16)])
        g2 = p.on("sync", lambda e: e.dma_start(out=gkn[0:1, 8:16], in_=gstat_sh[1:2, 0:8]),
                  inc=[("DX", 16)])
        p.on("sync", lambda e: e.dma_start(out=pmat_s[:, :], in_=pmatd), inc=[("DX", 16)])
        p.on("sync", lambda e: e.dma_start(out=cqf_s[:, :], in_=cqd), inc=[("DX", 16)])
        p.on("sync", lambda e: e.dma_start(out=sqf_s[:, :], in_=sqd), inc=[("DX", 16)])
        gm = p.on("vector",
                  lambda e: e.tensor_max(gkn[0:1, 0:8], gkn[0:1, 0:8], gkn[0:1, 8:16]),
                  wait=[("DX", g2["DX"])], inc=[("SV", 1)])

        y1f = y1_sh.flatten()
        rsf = r_stage.flatten()
        TSZ = 128 * D

        def rcopy(e):
            rr = None
            for j in range(8):
                with e.register("rj%d" % j) as rj:
                    e.reg_load(rj, ownv[0:1, j:j + 1])
                    off = e.snap(rj, min_val=0, max_val=T * D - TSZ)
                    rr = e.dma_start(out=rsf[j * TSZ:(j + 1) * TSZ],
                                     in_=y1f[bass.ds(off, TSZ)]).then_inc(p._sem("DW")[0], 16)
            return rr
        ent = p._sem("DW")
        ent[1] += 16 * 7
        rc = p.on("sync", rcopy, wait=[("DX", p.val("DX"))], inc=[("DW", 16)])
        rld = p.on("sync",
                   lambda e: e.dma_start(
                       out=Rr[:, :].rearrange("pp (m c) -> pp m c", m=8),
                       in_=r_stage[:, :].rearrange("(m pp) c -> pp m c", pp=128)),
                   wait=[("DW", rc["DW"])], inc=[("DX", 16)])
        p.flush("stageB_pre")

        st1 = [
            nc.sbuf_tensor("xnTo", [128, 16 * 1024], BF16),
            nc.sbuf_tensor("xnbB", [128, 2 * D], BF16),
            nc.sbuf_tensor("lnstB", [128, 16], F32),
            nc.sbuf_tensor("wqh0", [128, 16 * 128], BF16),
            nc.sbuf_tensor("wqh1", [128, 16 * 128], BF16),
            nc.sbuf_tensor("qbf", [128, 1024], BF16),
            nc.sbuf_tensor("pqb", [128, 1024], BF16),
            nc.sbuf_tensor("qt1", [128, 1024], BF16),
            nc.sbuf_tensor("sqq", [128, 1024], BF16),
            nc.sbuf_tensor("gnrm", [1, 1024], F32),
        ]
        (xnTo, xnbB, lnstB, wqh0, wqh1, qbf, pqb, qt1, sqq, gnrm) = \
            [t.__enter__() for t in st1]
        wqh = [wqh0, wqh1]

        xnb_freeB = {0: 0, 1: 0}
        bb = None
        for j in range(8):
            par = j % 2
            sc = emit_ln(p, Rr[:, j * D:(j + 1) * D], xnbB[:, par * D:(par + 1) * D], lnstB,
                         xnbB[:, par * D:(par + 1) * D], epsc,
                         ready=[("DX", rld["DX"])] + ([("DX", xnb_freeB[par])] if xnb_freeB[par] else []))
            bb = p.on("sync", lambda e, j=j, par=par: e.dma_start(
                out=xn_b[j * 128:(j + 1) * 128, :], in_=xnbB[:, par * D:(par + 1) * D]),
                wait=[("SC", sc)], inc=[("DX", 16)])
            xnb_freeB[par] = bb["DX"]
        for k in range(16):
            r = p.on("sync",
                     lambda e, k=k: e.dma_start_transpose(xnTo[:, k * 1024:(k + 1) * 1024],
                                                          xn_b[:, k * 128:(k + 1) * 128]),
                     wait=[("DX", bb["DX"])], inc=[("DX", 16)])
        xnTo_ready = ("DX", r["DX"])

        wq_wl = {}
        wq_free = {0: ("TM", 0), 1: ("TM", 0)}

        def load_wq(h):
            slot = wqh[h % 2]
            s_ = wq[:, h * 128:(h + 1) * 128].rearrange("(k pp) n -> pp k n", pp=128)
            d_ = slot[:, :].rearrange("pp (k n) -> pp k n", k=16)
            r = p.on("gpsimd", lambda e, s_=s_, d_=d_: e.dma_start(out=d_, in_=s_),
                     wait=[wq_free[h % 2]], inc=[("WL", 16)])
            wq_wl[h] = r["WL"]
        load_wq(0)
        load_wq(1)

        sp1_free = ("SC", 0)
        sqq_free = ("TM", 0)
        qn_done = []
        for h in range(16):
            g = h // 4

            def qmm(e, h=h):
                rr = None
                for n in range(2):
                    for k in range(16):
                        rr = e.matmul(SP0[:, n * 512:(n + 1) * 512],
                                      wqh[h % 2][:, k * 128:(k + 1) * 128],
                                      xnTo[:, k * 1024 + n * 512:k * 1024 + n * 512 + 512],
                                      start=(k == 0), stop=(k == 15))
                return rr
            t = p.on("tensor", qmm, wait=[("WL", wq_wl[h]), xnTo_ready, ("SC", p.val("SC"))],
                     inc=[("TM", 1)])
            wq_free[h % 2] = ("TM", t["TM"])
            if h + 2 < 16:
                load_wq(h + 2)
            c1 = p.on("scalar", lambda e: e.activation(qbf[:, :], SP0[:, :], AF.Copy),
                      wait=[("TM", t["TM"]), ("SV", p.val("SV"))], inc=[("SC", 1)])

            def pmm(e):
                e.matmul(SP1[:, 0:512], pmat_s[:, :], qbf[:, 0:512], start=True, stop=True)
                return e.matmul(SP1[:, 512:1024], pmat_s[:, :], qbf[:, 512:1024],
                                start=True, stop=True)
            t2 = p.on("tensor", pmm, wait=[("SC", c1["SC"]), sp1_free], inc=[("TM", 1)])
            c2 = p.on("scalar", lambda e: e.activation(pqb[:, :], SP1[:, :], AF.Copy),
                      wait=[("TM", t2["TM"]), ("SV", p.val("SV"))], inc=[("SC", 1)])
            v1 = p.on("vector", lambda e: e.tensor_mul(qt1[:, :], qbf[:, :], cqf_s[:, :]),
                      wait=[("SC", c1["SC"]), ("DX", p.val("DX"))], inc=[("SV", 1)])
            v2 = p.on("vector", lambda e: e.tensor_mul(pqb[:, :], pqb[:, :], sqf_s[:, :]),
                      wait=[("SC", c2["SC"])], inc=[("SV", 1)])
            v3 = p.on("vector",
                      lambda e, h=h, g=g: e.tensor_add(
                          qTg[:, g * 4096 + (h % 4) * 1024:g * 4096 + (h % 4) * 1024 + 1024],
                          qt1[:, :], pqb[:, :]),
                      wait=[("SV", v2["SV"]), ("SV", v1["SV"])], inc=[("SV", 1)])
            c3 = p.on("scalar",
                      lambda e, h=h, g=g: e.activation(
                          sqq[:, :],
                          qTg[:, g * 4096 + (h % 4) * 1024:g * 4096 + (h % 4) * 1024 + 1024],
                          AF.Square),
                      wait=[("SV", v3["SV"]), sqq_free], inc=[("SC", 1)])

            def nmm(e):
                e.matmul(SP1[0:1, 0:512], onesc[:, :], sqq[:, 0:512], start=True, stop=True)
                return e.matmul(SP1[0:1, 512:1024], onesc[:, :], sqq[:, 512:1024],
                                start=True, stop=True)
            tn = p.on("tensor", nmm, wait=[("SC", c3["SC"]), ("SC", c2["SC"])], inc=[("TM", 1)])
            sqq_free = ("TM", tn["TM"])
            if h % 4 == 0:
                nv = p.on("vector", lambda e: e.tensor_copy(gnrm[0:1, :], SP1[0:1, :]),
                          wait=[("TM", tn["TM"])], inc=[("SV", 1)])
            else:
                nv = p.on("vector",
                          lambda e: e.tensor_max(gnrm[0:1, :], gnrm[0:1, :], SP1[0:1, :]),
                          wait=[("TM", tn["TM"]), ("SV", p.val("SV"))], inc=[("SV", 1)])
            sp1_free = ("SV", nv["SV"])
            if h % 4 == 3:
                prev = nv
                for j in range(8):
                    prev = p.on("vector",
                                lambda e, j=j, g=g: e.reduce_max(
                                    srow[0:1, g * 8 + j:g * 8 + j + 1],
                                    gnrm[0:1, j * 128:(j + 1) * 128], AX.X),
                                wait=[("SV", prev["SV"])], inc=[("SV", 1)])
                qn_done.append(prev["SV"])

        mh0 = p.on("vector",
                   lambda e: e.tensor_mul(
                       srow[0:1, 0:32].rearrange("p (g j) -> p g j", g=4),
                       srow[0:1, 0:32].rearrange("p (g j) -> p g j", g=4),
                       gkn[0:1, 0:4].unsqueeze(2).broadcast_to((1, 4, 8))),
                   wait=[("SV", qn_done[-1]), ("SV", gm["SV"])], inc=[("SV", 1)])
        mh1 = p.on("scalar", lambda e: e.activation(srow[0:1, 32:64], srow[0:1, 0:32], AF.Sqrt),
                   wait=[("SV", mh0["SV"])], inc=[("SC", 1)])
        mh2 = p.on("vector", lambda e: e.tensor_scalar_mul(srow[0:1, 0:32], srow[0:1, 32:64], -1.0),
                   wait=[("SC", mh1["SC"])], inc=[("SV", 1)])
        mh3 = p.on("vector",
                   lambda e: e.tensor_copy(srow[0:1, 32:48].bitcast(BF16)[0:1, 0:32],
                                           srow[0:1, 0:32]),
                   wait=[("SV", mh2["SV"])], inc=[("SV", 1)])

        def bcm(e):
            return e.matmul(SP0[:, 0:32], onesr[0:1, :], srow[0:1, 32:48].bitcast(BF16)[0:1, 0:32],
                            start=True, stop=True)
        tb = p.on("tensor", bcm, wait=[("SV", mh3["SV"]), ("SC", p.val("SC"))], inc=[("TM", 1)])
        bv = p.on("vector", lambda e: e.tensor_copy(negmv[:, :], SP0[:, 0:32]),
                  wait=[("TM", tb["TM"])], inc=[("SV", 1)])
        p.flush("stageB_qproj")

        for t in reversed(st1):
            t.__exit__(None, None, None)

        st2 = [
            nc.sbuf_tensor("kwin", [128, 2 * 1024], BF16),
            nc.sbuf_tensor("vwin", [128, 2 * 1024], BF16),
            nc.sbuf_tensor("ptb", [128, 2 * 1024], BF16),
            nc.sbuf_tensor("maskb", [128, 4096], BF16),
            nc.sbuf_tensor("wos", [128, 16 * 512], BF16),
            nc.sbuf_tensor("rrow", [1, 512], F32),
            nc.sbuf_tensor("rrbf", [1, 512], BF16),
            nc.sbuf_tensor("recb", [128, 512], F32),
        ]
        (kwin, vwin, ptb, maskb, wos, rrow, rrbf, recb) = [t.__enter__() for t in st2]

        kv_free = {0: ("TM", 0), 1: ("TM", 0)}
        pt_free = {0: ("TM", 0), 1: ("TM", 0)}
        sp_free = {0: ("SC", 0), 1: ("SC", 0)}
        l_free = {0: ("SV", 0), 1: ("SV", 0)}
        mask_rd = ("SV", 0)
        ot_free = ("SV", 0)
        qr = qTg[:, :].rearrange("p (g hh t) -> p g hh t", g=4, hh=4)
        pair_idx = 0
        for j in range(8):
            nw = LJ[j] // 8
            ml = p.on("sync", lambda e, j=j: e.dma_start(out=maskb[:, :], in_=bmask[j]),
                      wait=[mask_rd], inc=[("DX", 16)])
            for g in range(4):
                lp = (j * 4 + g) % 2
                Lx = Lp[lp]
                lw = list([l_free[lp]])
                first = True
                l_last = None
                ot_last = None
                for w in range(nw):
                    par = w % 2
                    hb, qb = w // 4, w % 4
                    kl = p.on("sync",
                              lambda e, hb=hb, qb=qb, g=g, par=par: e.dma_start(
                                  out=kwin[:, par * 1024:(par + 1) * 1024],
                                  in_=kT_sh[hb, qb, g * 128:(g + 1) * 128, :]),
                              wait=[kv_free[par]], inc=[("DX", 16)])
                    vl = p.on("sync",
                              lambda e, hb=hb, qb=qb, g=g, par=par: e.dma_start(
                                  out=vwin[:, par * 1024:(par + 1) * 1024].rearrange(
                                      "p (t c) -> p t c", t=8),
                                  in_=v_sh[hb, qb, :, g * 128:(g + 1) * 128].rearrange(
                                      "(t p) c -> p t c", p=128)),
                              wait=[kv_free[par]], inc=[("DX", 16)])
                    boundary = (w == nw - 1)
                    for i in range(4):
                        sp = pair_idx % 2
                        ptp = pair_idx % 2
                        pair_idx += 1
                        SPx = SP[sp]

                        def smm(e, i=i, par=par, g=g, j=j, SPx=SPx):
                            rr = None
                            for half in range(2):
                                tr = 2 * i + half
                                rr = e.matmul(SPx[:, half * 512:(half + 1) * 512],
                                              kwin[:, par * 1024 + tr * 128:par * 1024 + (tr + 1) * 128],
                                              qr[:, g, :, j * 128:(j + 1) * 128],
                                              start=True, stop=True)
                            return rr
                        t = p.on("tensor", smm,
                                 wait=[("DX", kl["DX"]), sp_free[sp], ("SV", bv["SV"])],
                                 inc=[("TM", 1)])
                        ce = p.on("scalar",
                                  lambda e, ptp=ptp, g=g, j=j, SPx=SPx: e.activation(
                                      ptb[:, ptp * 1024:(ptp + 1) * 1024], SPx[:, :], AF.Exp,
                                      bias=negmv[:, g * 8 + j:g * 8 + j + 1]),
                                  wait=[("TM", t["TM"]), pt_free[ptp]], inc=[("SC", 1)])
                        sp_free[sp] = ("SC", ce["SC"])
                        avdep = [("SC", ce["SC"])]
                        if boundary:
                            mv = p.on("vector",
                                      lambda e, ptp=ptp, i=i: e.tensor_mul(
                                          ptb[:, ptp * 1024:(ptp + 1) * 1024],
                                          ptb[:, ptp * 1024:(ptp + 1) * 1024],
                                          maskb[:, i * 1024:(i + 1) * 1024]),
                                      wait=[("SC", ce["SC"]), ("DX", ml["DX"])], inc=[("SV", 1)])
                            avdep = [("SV", mv["SV"])]

                        def avmm(e, i=i, par=par, ptp=ptp, first=first):
                            rr = None
                            for half in range(2):
                                tr = 2 * i + half
                                rr = e.matmul(OT[:, :],
                                              vwin[:, par * 1024 + tr * 128:par * 1024 + (tr + 1) * 128],
                                              ptb[:, ptp * 1024 + half * 512:ptp * 1024 + (half + 1) * 512],
                                              start=(first and half == 0), stop=False)
                            return rr
                        ta = p.on("tensor", avmm, wait=avdep + [("DX", vl["DX"]), ot_free],
                                  inc=[("TM", 1)])

                        def lmm(e, ptp=ptp, Lx=Lx, first=first):
                            rr = None
                            for half in range(2):
                                rr = e.matmul(Lx[0:1, :], onesc[:, :],
                                              ptb[:, ptp * 1024 + half * 512:ptp * 1024 + (half + 1) * 512],
                                              start=(first and half == 0), stop=False)
                            return rr
                        tl = p.on("tensor", lmm, wait=list(lw), inc=[("TM", 1)])
                        lw = []
                        first = False
                        pt_free[ptp] = ("TM", tl["TM"])
                        l_last = tl["TM"]
                        ot_last = ta["TM"]
                    kv_free[par] = ("TM", p.val("TM"))
                mask_rd = ("SV", p.val("SV")) if boundary else mask_rd
                e0 = p.on("vector", lambda e, Lx=Lx: e.reciprocal(rrow[0:1, :], Lx[0:1, :]),
                          wait=[("TM", l_last)], inc=[("SV", 1)])
                l_free[lp] = ("SV", e0["SV"])
                e1 = p.on("vector", lambda e: e.tensor_copy(rrbf[0:1, :], rrow[0:1, :]),
                          wait=[("SV", e0["SV"])], inc=[("SV", 1)])

                def rbm(e):
                    return e.matmul(SP0[:, 0:512], onesr[0:1, :], rrbf[0:1, :],
                                    start=True, stop=True)
                t = p.on("tensor", rbm, wait=[("SV", e1["SV"]), sp_free[0]], inc=[("TM", 1)])
                e2 = p.on("vector", lambda e: e.tensor_copy(recb[:, :], SP0[:, 0:512]),
                          wait=[("TM", t["TM"])], inc=[("SV", 1)])
                sp_free[0] = ("SV", e2["SV"])
                e3 = p.on("vector",
                          lambda e, j=j, g=g: e.tensor_mul(
                              outT[:, :].rearrange("p (gh t) -> p gh t", gh=16)[
                                  :, g * 4:(g + 1) * 4, j * 128:(j + 1) * 128],
                              OT[:, :].rearrange("p (hh t) -> p hh t", hh=4),
                              recb[:, :].rearrange("p (hh t) -> p hh t", hh=4)),
                          wait=[("SV", e2["SV"]), ("TM", ot_last)], inc=[("SV", 1)])
                ot_free = ("SV", e3["SV"])
        flash_done = p.val("SV")

        wo_wl = {}
        oslot_free = {0: ("SV", 0), 1: ("SV", 0), 2: ("SV", 0), 3: ("SV", 0)}
        wos_free = ("TM", 0)
        for nd in range(4):
            s_ = wo[:, nd * 512:(nd + 1) * 512].rearrange("(k pp) n -> pp k n", pp=128)
            d_ = wos[:, :].rearrange("pp (k n) -> pp k n", k=16)
            lwo = p.on("gpsimd", lambda e, s_=s_, d_=d_: e.dma_start(out=d_, in_=s_),
                       wait=[wos_free], inc=[("WL", 16)])
            for m in range(8):
                slot = (m + nd) % 4
                pso = SP[slot // 2][:, (slot % 2) * 512:(slot % 2 + 1) * 512]

                def omm(e, m=m, pso=pso):
                    rr = None
                    for k in range(16):
                        rr = e.matmul(pso, outT[:, k * 1024 + m * 128:k * 1024 + (m + 1) * 128],
                                      wos[:, k * 512:(k + 1) * 512],
                                      start=(k == 0), stop=(k == 15))
                    return rr
                t = p.on("tensor", omm,
                         wait=[("WL", lwo["WL"]), oslot_free[slot], ("SV", flash_done)],
                         inc=[("TM", 1)])
                v = p.on("vector",
                         lambda e, m=m, nd=nd, pso=pso: e.tensor_add(
                             Rr[:, m * D + nd * 512:m * D + nd * 512 + 512],
                             Rr[:, m * D + nd * 512:m * D + nd * 512 + 512], pso),
                         wait=[("TM", t["TM"])], inc=[("SV", 1)])
                oslot_free[slot] = ("SV", v["SV"])
            wos_free = ("TM", p.val("TM"))
        p.flush("stageB_flash")
        for t in reversed(st2):
            t.__exit__(None, None, None)
        for t in reversed(psB):
            t.__exit__(None, None, None)
        for t in reversed(stB):
            t.__exit__(None, None, None)
        if debug:
            p.on("sync", lambda e: e.dma_start(out=dbg["y2"].rearrange("(m pp) c -> pp m c", pp=128),
                                               in_=Rr[:, :].rearrange("pp (m c) -> pp m c", m=8)),
                 wait=[("SV", p.val("SV"))], inc=[("DX", 16)])

    if stages >= 3:
        stC = [
            nc.sbuf_tensor("xnbC", [128, 2 * D], BF16),
            nc.sbuf_tensor("xnTC", [128, 16 * 1024], BF16),
            nc.sbuf_tensor("lnstC", [128, 16], F32),
            nc.sbuf_tensor("rtC", [128, 40], F32),
            nc.sbuf_tensor("cbufC", [128, 64], F32),
            nc.sbuf_tensor("wAC", [128, 16 * 1024], BF16),
            nc.sbuf_tensor("wBC", [128, 16 * 1024], BF16),
            nc.sbuf_tensor("w3s0C", [128, 8 * 256], BF16),
            nc.sbuf_tensor("w3s1C", [128, 8 * 256], BF16),
            nc.sbuf_tensor("s1C", [128, 8 * 512], BF16),
            nc.sbuf_tensor("hpC", [128, 8 * 1024], BF16),
        ]
        (xnbC, xnTC, lnstC, rtC, cbufC, wAC, wBC, w3s0C, w3s1C, s1C, hpC) = \
            [t.__enter__() for t in stC]
        psC = [
            nc.psum_tensor("ph0C", [128, 512], F32),
            nc.psum_tensor("ph1C", [128, 512], F32),
            nc.psum_tensor("po0C", [128, 512], F32),
            nc.psum_tensor("po1C", [128, 512], F32),
            nc.psum_tensor("po2C", [128, 512], F32),
            nc.psum_tensor("po3C", [128, 512], F32),
        ]
        ph0C, ph1C, po0C, po1C, po2C, po3C = [t.__enter__() for t in psC]
        sbC = dict(wA=wAC, wB=wBC, w3s0=w3s0C, w3s1=w3s1C, s1=s1C, hp=hpC)
        psdC = dict(h0=ph0C, h1=ph1C, o=[po0C, po1C, po2C, po3C])

        xnb_freeC = {0: 0, 1: 0}
        bb = None
        for m in range(8):
            par = m % 2
            sc = emit_ln(p, Rr[:, m * D:(m + 1) * D], xnbC[:, par * D:(par + 1) * D], lnstC,
                         xnbC[:, par * D:(par + 1) * D], epsc,
                         ready=([("SV", p.val("SV"))] if m == 0 else []) +
                               ([("DX", xnb_freeC[par])] if xnb_freeC[par] else []))
            bb = p.on("sync", lambda e, m=m, par=par: e.dma_start(
                out=xn_b[m * 128:(m + 1) * 128, :], in_=xnbC[:, par * D:(par + 1) * D]),
                wait=[("SC", sc)], inc=[("DX", 16)])
            xnb_freeC[par] = bb["DX"]
        for k in range(16):
            r = p.on("sync",
                     lambda e, k=k: e.dma_start_transpose(xnTC[:, k * 1024:(k + 1) * 1024],
                                                          xn_b[:, k * 128:(k + 1) * 128]),
                     wait=[("DX", bb["DX"])], inc=[("DX", 16)])
        xnTC_ready = [("DX", r["DX"])]
        prev_read = []
        for m in range(8):
            cv, prev_read = emit_routing(p, xnTC, rwT, cbufC, rtC, po0C, m, xnTC_ready, prev_read)
        c_ready = [("SV", cv)]
        emit_moe(p, sbC, psdC, xnTC, cbufC, Rr, w1d, w2d, w3d, xnTC_ready, c_ready, prev_read[0])
        done = p.val("SV")
        for m in range(8):
            p.on("sync", lambda e, m=m: e.dma_start(out=yout[m * 128:(m + 1) * 128, :],
                                                    in_=Rr[:, m * D:(m + 1) * D]),
                 wait=([("SV", done)] if m == 0 else []), inc=[("ST", 16)])
        p.flush("stageC")
        for t in reversed(psC):
            t.__exit__(None, None, None)
        for t in reversed(stC):
            t.__exit__(None, None, None)
    if stages >= 2:
        for t in reversed(stB_R):
            t.__exit__(None, None, None)

    p.on("sync", lambda e: e.nop(),
         wait=[("DX", p.val("DX")), ("DW", p.val("DW")), ("BAR", 1), ("ST", p.val("ST")),
               ("SV", p.val("SV")), ("SC", p.val("SC")), ("TM", p.val("TM"))])
    p.flush("tail")

    for t in reversed(pers):
        t.__exit__(None, None, None)
    return nc


def make_inputs(c, x_full, W, nonce_val):
    h = c % 2
    own = own_tiles(c)
    own_pos = np.concatenate([np.arange(t * 128, (t + 1) * 128) for t in own])
    cq, sq_ = rope_tables(own_pos, scale=1.0 / np.sqrt(HD))
    ckk, skk = rope_tables(np.arange(h * HALF, (h + 1) * HALF))
    consts = np.zeros((128, 130), dtype=np.float32)
    consts[:, 0:128] = np.eye(128)
    consts[:, 128] = 1.0
    return {
        "x_half": np.ascontiguousarray(x_full[h * HALF:(h + 1) * HALF]),
        "wq": W["wq"], "wk": W["wk"], "wv": W["wv"], "wo": W["wo"],
        "rw": W["rw"], "w1": W["w1"], "w2": W["w2"], "w3": W["w3"],
        "cosk": ckk, "sink": skk, "cosq": cq, "sinq": sq_,
        "bmask": boundary_masks(c),
        "pmat": pmat_host(),
        "cqf": np.ascontiguousarray(cq.astype(BF)),
        "sqf": np.ascontiguousarray(sq_.astype(BF)),
        "consts": consts.astype(BF),
        "nonce": np.array([[nonce_val]], dtype=np.int32),
        "ownt": np.array([own], dtype=np.int32),
        "ownoff": np.array([[t * 128 * D for t in own]], dtype=np.int32),
        "halfsel": np.array([[h]], dtype=np.int32),
    }


_NC_CACHE = {}


def _get_program():
    if "nc" not in _NC_CACHE:
        _NC_CACHE["nc"] = build_program(stages=3, debug=False)
    return _NC_CACHE["nc"]


def kernel(L0_attn_x, L0_attn_wq, L0_attn_wk, L0_attn_wv, L0_attn_wo,
           L0_ffn_router_w, L0_ffn_w1, L0_ffn_w2, L0_ffn_w3):
    x_full = np.ascontiguousarray(np.asarray(L0_attn_x, dtype=np.float32)[0])
    W = dict(wq=np.ascontiguousarray(np.asarray(L0_attn_wq, dtype=np.float32)),
             wk=np.ascontiguousarray(np.asarray(L0_attn_wk, dtype=np.float32)),
             wv=np.ascontiguousarray(np.asarray(L0_attn_wv, dtype=np.float32)),
             wo=np.ascontiguousarray(np.asarray(L0_attn_wo, dtype=np.float32)),
             rw=np.ascontiguousarray(np.asarray(L0_ffn_router_w, dtype=np.float32)),
             w1=np.ascontiguousarray(np.asarray(L0_ffn_w1, dtype=np.float32)),
             w2=np.ascontiguousarray(np.asarray(L0_ffn_w2, dtype=np.float32)),
             w3=np.ascontiguousarray(np.asarray(L0_ffn_w3, dtype=np.float32)))
    nc = _get_program()
    nonce = int(np.random.randint(1, 2 ** 30))
    in_maps = [make_inputs(c, x_full, W, nonce) for c in range(8)]
    res = run_bass_kernel_spmd(nc, in_maps, core_ids=list(range(8)))
    out = np.empty((1, T, D), dtype=np.float32)
    for c in range(8):
        yo = res.results[c]["yout"]
        for j, t in enumerate(own_tiles(c)):
            out[0, t * 128:(t + 1) * 128, :] = yo[j * 128:(j + 1) * 128, :]
    return out
